# version 10
# baseline (speedup 1.0000x reference)
import math
from contextlib import ExitStack

import numpy as np
import ml_dtypes
import concourse.bass as bass
import concourse.mybir as mybir
from concourse.alu_op_type import AluOpType
from concourse.bass_utils import run_bass_kernel_spmd

F32 = mybir.dt.float32
BF16 = mybir.dt.bfloat16
AF = mybir.ActivationFunctionType
AX = mybir.AxisListType
OP = AluOpType

ENGINES = ("sync", "pe", "act", "dve", "pool")
_ENG_ATTR = {"sync": "sync", "pe": "tensor", "act": "scalar", "dve": "vector",
             "pool": "gpsimd"}
SEM_CAP = 24000


class _Op:
    __slots__ = ("eng", "fn", "idx", "inc_slot", "deps", "is_dma", "tag")

    def __init__(self, eng, fn, idx, deps, is_dma, tag):
        self.eng, self.fn, self.idx = eng, fn, idx
        self.deps, self.is_dma, self.tag = deps, is_dma, tag
        self.inc_slot = None


class Plan:

    def __init__(self, nc):
        self.nc = nc
        self.ops = {e: [] for e in ENGINES}
        self.last_writer = {}
        self.readers = {}
        self._sem_pool = []
        self.global_after = None

    def give_sems(self, handles):
        self._sem_pool = list(handles)

    def op(self, eng, fn, reads=(), writes=(), after=(), is_dma=False, tag=""):
        deps = set()
        for b in reads:
            w = self.last_writer.get(b)
            if w is not None:
                deps.add(w)
        for b in writes:
            w = self.last_writer.get(b)
            if w is not None:
                deps.add(w)
            for r in self.readers.get(b, ()):
                deps.add(r)
        for d in after:
            if d is not None:
                deps.add(d)
        if self.global_after is not None:
            deps.add(self.global_after)
        o = _Op(eng, fn, len(self.ops[eng]), deps, is_dma, tag)
        self.ops[eng].append(o)
        for b in reads:
            self.readers.setdefault(b, []).append(o)
        for b in writes:
            self.last_writer[b] = o
            self.readers[b] = []
        return o

    def dma(self, eng, fn, reads=(), writes=(), after=(), tag=""):
        return self.op(eng, fn, reads, writes, after, is_dma=True, tag=tag)

    def lower(self, block):
        needed = set()
        for e in ENGINES:
            for o in self.ops[e]:
                if o.is_dma:
                    needed.add(o)
                for d in o.deps:
                    if d.eng != o.eng or d.is_dma:
                        needed.add(d)
        sem_iter = iter(self._sem_pool)
        for e in ENGINES:
            for dma_class in (False, True):
                cnt = 0
                cur = None
                for o in self.ops[e]:
                    if o in needed and o.is_dma == dma_class:
                        step = 16 if o.is_dma else 1
                        if cur is None or cnt + step > SEM_CAP:
                            cur = next(sem_iter)
                            cnt = 0
                        cnt += step
                        o.inc_slot = (cur, cnt)

        def emitter(eng):
            def run(engine_obj):
                waited = {}
                for o in self.ops[eng]:
                    for d in sorted(o.deps, key=lambda x: (x.eng, x.idx)):
                        if d.eng == eng and not d.is_dma:
                            continue
                        sem, val = d.inc_slot
                        k = id(sem)
                        if waited.get(k, -1) >= val:
                            continue
                        engine_obj.wait_ge(sem, val)
                        waited[k] = val
                    inst = o.fn(engine_obj)
                    if o.inc_slot is not None:
                        sem, _ = o.inc_slot
                        inst.then_inc(sem, 16 if o.is_dma else 1)
            return run

        for e in ENGINES:
            if self.ops[e]:
                getattr(block, _ENG_ATTR[e])(emitter(e))


B_TOT, T, C = 16, 2048, 1024
NCORES = 8
B_LOC = B_TOT // NCORES
NCH = C // 128
H_MLA, LAT, HD = 16, 32, 64
I_DH, I_WIN = 64, 256
I_DEPTH = 6
DI = 2730
NJ = 22
QK_SCALE = 8.0
NEG = -1e30
NTT = T // 128
NST = T // 512
EPS = 1e-5


def _host_consts():
    out = {}
    out["ident"] = np.eye(128, dtype=np.float32)
    t = np.arange(T, dtype=np.float64)
    inv = 1.0 / 10000.0 ** (np.arange(0, LAT, 2, dtype=np.float64) / LAT)
    fr = np.outer(t, inv)
    cos = np.repeat(np.cos(fr), 2, axis=-1)[:, :LAT]
    sin = np.repeat(np.sin(fr), 2, axis=-1)[:, :LAT]
    sp = np.empty_like(sin)
    sp[:, 1::2] = sin[:, 0::2]
    sp[:, 0::2] = sin[:, 1::2]
    out["mcos"] = np.tile(cos.T, (4, 1)).astype(np.float32)
    out["msinp"] = np.tile(sp.T, (4, 1)).astype(np.float32)
    R = np.zeros((LAT, LAT))
    for i in range(LAT // 2):
        R[2 * i, 2 * i + 1] = -1.0
        R[2 * i + 1, 2 * i] = 1.0
    RT = np.zeros((64, 64))
    RT[:32, :32] = R.T
    RT[32:, 32:] = R.T
    out["mrotT"] = RT.astype(np.float32)
    half = np.arange(0, I_DH, 2, dtype=np.float64)
    invf = 1.0 / 10000.0 ** (half / I_DH)
    fr = t[:, None] * invf[None, :]
    fr = np.concatenate([fr, fr], -1)
    base = (half + 0.4 * I_DH) / (1.4 * I_DH)
    power = (t - T // 2) / (I_WIN // 2)
    scale = base[None, :] ** power[:, None]
    scale = np.concatenate([scale, scale], -1)
    cos_i, sin_i = np.cos(fr), np.sin(fr)

    def perm_half(s):
        sp = np.empty_like(s)
        sp[:, 32:] = s[:, :32]
        sp[:, :32] = s[:, 32:]
        return sp

    out["qcos"] = np.tile((cos_i * scale).T, (2, 1)).astype(np.float32)
    out["qsinp"] = np.tile(perm_half(sin_i * scale).T, (2, 1)).astype(np.float32)
    out["kcos"] = np.tile((cos_i / scale).T, (2, 1)).astype(np.float32)
    out["ksinp"] = np.tile(perm_half(sin_i / scale).T, (2, 1)).astype(np.float32)
    R64 = np.zeros((64, 64))
    for d in range(32):
        R64[d, d + 32] = -1.0
        R64[d + 32, d] = 1.0
    RT = np.zeros((128, 128))
    RT[:64, :64] = R64.T
    RT[64:, 64:] = R64.T
    out["irotT"] = RT.astype(np.float32)
    r = np.arange(128)[:, None]
    c = np.arange(128)[None, :]
    out["maskC"] = np.where(c <= r, 0.0, NEG).astype(np.float32)
    out["maskA"] = np.where(c >= r, 0.0, NEG).astype(np.float32)
    out["onescol"] = np.ones((128, 1), dtype=np.float32)
    out["onesrow"] = np.ones((1, 128), dtype=np.float32)
    E4 = np.zeros((4, 128), dtype=np.float32)
    for h in range(4):
        E4[h, 32 * h:32 * h + 32] = 1.0
    out["E4"] = E4
    E2 = np.zeros((2, 128), dtype=np.float32)
    E2[0, :64] = 1.0
    E2[1, 64:] = 1.0
    out["E2"] = E2
    E16g = np.zeros((16, 512), dtype=np.float32)
    for j in range(4):
        for m in range(128):
            E16g[4 * j + m // 32, 128 * j + m] = 1.0
    out["E16g"] = E16g
    P2 = np.zeros((128, 2), dtype=np.float32)
    P2[:64, 0] = 1.0
    P2[64:, 1] = 1.0
    out["ones2"] = P2
    return out


def _relayout_inputs(inp, core):
    f = np.float32
    d = {}
    d["x"] = np.ascontiguousarray(inp["L0_attn_x"][2 * core:2 * core + 2], f)
    d["cattn"] = np.ascontiguousarray(inp["L0_attn_c_attn_w"], f)
    d["cproj"] = np.ascontiguousarray(inp["L0_attn_cproj_w"], f)
    bd = np.zeros((128, 64), f)
    bd[:64, :32] = inp["L0_attn_q2l_w"].T
    bd[64:, 32:] = inp["L0_attn_q2l_w"].T
    d["q2l_bd"] = bd
    bd = np.zeros((128, 64), f)
    bd[:64, :32] = inp["L0_attn_kv2l_w"].T
    bd[64:, 32:] = inp["L0_attn_kv2l_w"].T
    d["kv2l_bd"] = bd
    bd = np.zeros((64, 64), f)
    bd[:32, :32] = inp["L0_attn_wqk_w"].T
    bd[32:, 32:] = inp["L0_attn_wqk_w"].T
    d["wqk_bd"] = bd
    l2o = inp["L0_attn_l2o_w"]
    bd = np.zeros((128, 256), f)
    for hh in range(2):
        bd[32 * hh:32 * hh + 32, 64 * hh:64 * hh + 64] = l2o.T
        bd[64 + 32 * hh:96 + 32 * hh, 128 + 64 * hh:192 + 64 * hh] = l2o.T
    d["l2o_bd"] = bd
    d["cattn_b_t"] = np.ascontiguousarray(
        np.asarray(inp["L0_attn_c_attn_b"]).reshape(24, 128).T, f)
    d["q2l_b_t"] = np.asarray(np.tile(inp["L0_attn_q2l_b"], 4)[:, None], f)
    d["kv2l_b_t"] = np.asarray(np.tile(inp["L0_attn_kv2l_b"], 4)[:, None], f)
    d["wqk_b_t"] = np.asarray(np.tile(inp["L0_attn_wqk_b"], 4)[:, None], f)
    d["kv2l_b_row"] = np.asarray(np.tile(inp["L0_attn_kv2l_b"], 16)[None, :], f)
    d["l2o_b_t"] = np.asarray(np.tile(inp["L0_attn_l2o_b"], 2)[:, None], f)
    d["cproj_b_t"] = np.ascontiguousarray(
        np.asarray(inp["L0_attn_cproj_b"]).reshape(8, 128).T, f)
    wqkv = np.asarray(inp["L0_ffn_Wqkv"], f)
    d["wqkv_p"] = np.ascontiguousarray(
        wqkv.reshape(6, 8, 128, 6, 512).transpose(0, 3, 2, 1, 4)
        .reshape(6, 6, 128, 8 * 512))
    wo = np.asarray(inp["L0_ffn_Wo"], f)
    d["wo_p"] = np.ascontiguousarray(
        wo.reshape(6, 8, 128, 1024).transpose(0, 2, 1, 3).reshape(6, 128, 8192))
    w1 = np.asarray(inp["L0_ffn_W1"], f)
    w1p = np.zeros((6, NJ, 128, 2048), f)
    for j in range(NJ):
        aw = min(128, DI - 128 * j)
        a_cols = w1[:, :, 128 * j:128 * j + aw]
        g_cols = w1[:, :, DI + 128 * j:DI + 128 * j + aw]
        w1p[:, j, :, :].reshape(6, 128, 8, 256)[:, :, :, :aw] = \
            a_cols.reshape(6, 8, 128, aw).transpose(0, 2, 1, 3)
        w1p[:, j, :, :].reshape(6, 128, 8, 256)[:, :, :, 128:128 + aw] = \
            g_cols.reshape(6, 8, 128, aw).transpose(0, 2, 1, 3)
    d["w1_p"] = np.ascontiguousarray(w1p)
    w2 = np.asarray(inp["L0_ffn_W2"], f)
    w2p = np.zeros((6, 8, 128, NJ * 128), f)
    for kc in range(NJ):
        r = min(128, DI - 128 * kc)
        blk = w2[:, 128 * kc:128 * kc + r, :]
        w2p[:, :, :r, 128 * kc:128 * kc + 128] = \
            blk.reshape(6, r, 8, 128).transpose(0, 2, 1, 3)
    d["w2_p"] = np.ascontiguousarray(w2p)
    d["qscale_t"] = np.ascontiguousarray(
        np.tile(np.asarray(inp["L0_ffn_q_scale"], f), (1, 2))[:, :, None])
    d["kscale_t"] = np.ascontiguousarray(
        np.tile(np.asarray(inp["L0_ffn_k_scale"], f), (1, 2))[:, :, None])
    d["lng_t"] = np.ascontiguousarray(
        np.asarray(inp["L0_ffn_ln_g"], f).reshape(6, 8, 128).transpose(0, 2, 1))
    d["lnb_t"] = np.ascontiguousarray(
        np.asarray(inp["L0_ffn_ln_b"], f).reshape(6, 8, 128).transpose(0, 2, 1))
    b1 = np.asarray(inp["L0_ffn_b1"], f)
    b1a = np.zeros((6, NJ, 128), f)
    b1g = np.zeros((6, NJ, 128), f)
    for j in range(NJ):
        aw = min(128, DI - 128 * j)
        b1a[:, j, :aw] = b1[:, 128 * j:128 * j + aw]
        b1g[:, j, :aw] = b1[:, DI + 128 * j:DI + 128 * j + aw]
    d["b1a_t"] = np.ascontiguousarray(b1a.transpose(0, 2, 1))
    d["b1g_t"] = np.ascontiguousarray(b1g.transpose(0, 2, 1))
    d["b2_t"] = np.ascontiguousarray(
        np.asarray(inp["L0_ffn_b2"], f).reshape(6, 8, 128).transpose(0, 2, 1))
    for k, v in _host_consts().items():
        d[k] = v
    return {k: np.ascontiguousarray(v, dtype=np.float32) for k, v in d.items()}



AB_SZ = 43792
AF_SZ = 8192


def _gran(pre, off, cols, g=512):
    return [(pre, i) for i in range(off // g, (off + cols + g - 1) // g)]


class K:
    @staticmethod
    def bf(off, cols):
        assert 0 <= off and off + cols <= AB_SZ, (off, cols)
        return _gran("abf", off, cols)

    @staticmethod
    def f(off, cols):
        assert 0 <= off and off + cols <= AF_SZ, (off, cols)
        return _gran("af32", off, cols)

    @staticmethod
    def x(off, cols):
        return _gran("xbuf", off, cols)

    @staticmethod
    def ps(name, off=0, cols=512):
        return _gran("ps:" + name, off, cols, 512)


MSC = 1.0 / math.sqrt(LAT)
ISC = QK_SCALE

R_QCOS, R_QSINP, R_KCOS, R_KSINP = 34560, 36608, 38656, 40704
R_IDENT, R_MASKC, R_MASKA, R_IROT = 42752, 42880, 43008, 43136
R_MROT, R_Q2L, R_KV2L, R_WQK, R_L2O = 43264, 43328, 43392, 43456, 43520
R_ONESC, R_ONES2 = 43776, 43778
M_H1, M_WST, M_WVL, M_QKA, M_QKB = 0, 4096, 5120, 9216, 9728
M_CPT, M_PBF, M_PTB, M_YLAT, M_YBF = 0, 8192, 10240, 10752, 11264
M_QF, M_KL, M_VL = 12288, 20480, 28672
I_INBF, I_QB, I_KB, I_VB, I_WO, I_WQKV, I_PFB, I_PTB, I_MISC = \
    0, 4096, 8192, 14336, 20480, 28672, 32768, 33792, 34304
F_XN, F_H1G, F_W1, F_GELU, F_W2A, F_W2B = 0, 8192, 30720, 32768, 0, 2816
F_SQ = 30720
P_A, P_B, P_W1A, P_W1B, P_W2A, P_W2B, P_CT, P_CPJ = \
    0, 4096, 8192, 10240, 12288, 13696, 15104, 17152
P_CT2 = 16128
A_ST0, A_ST1, A_ASB, A_QLAT, A_TC = 0, 1024, 2048, 2048, 2560
A_G1, A_G2, A_G3, A_RB = 6144, 6656, 7168, 7680
BC_Q2L, BC_KV2L, BC_WQK, BC_L2O, BC_CPROJ, BC_CATTN = 0, 1, 2, 3, 4, 12
BC_QS, BC_KS, BC_LNG, BC_LNB, BC_B1A, BC_B1G, BC_B2 = 40, 41, 42, 50, 58, 80, 102
SC_INMAX, SC_INBIAS, SC_INSUM, SC_INRS = 16, 20, 24, 28
SC_RUN, SC_TMP, SC_B4, SC_ACCT, SC_SALL, SC_ST = 48, 52, 56, 64, 80, 224


class Bld:
    def __init__(self, nc, P, depth=I_DEPTH, debug=None):
        self.nc, self.P = nc, P
        self.depth, self.debug = depth, debug
        self.dram = {}
        self.sb = {}
        self.ps = {}
        self.outdmas = []

    def declare(self):
        nc = self.nc
        D = self.dram
        D["x"] = nc.dram_tensor("x", [B_LOC, T, C], F32, kind="ExternalInput")
        D["out"] = nc.dram_tensor("out", [B_LOC, T, C], F32,
                                  kind="ExternalOutput")
        shapes = [
            ("cattn", [3072, 1024]), ("cproj", [1024, 1024]),
            ("q2l_bd", [128, 64]), ("kv2l_bd", [128, 64]),
            ("wqk_bd", [64, 64]), ("l2o_bd", [128, 256]),
            ("cattn_b_t", [128, 24]), ("q2l_b_t", [128, 1]),
            ("kv2l_b_t", [128, 1]), ("wqk_b_t", [128, 1]),
            ("kv2l_b_row", [1, 512]), ("l2o_b_t", [128, 1]),
            ("cproj_b_t", [128, 8]),
            ("wqkv_p", [6, 6, 128, 4096]), ("wo_p", [6, 128, 8192]),
            ("w1_p", [6, NJ, 128, 2048]), ("w2_p", [6, 8, 128, NJ * 128]),
            ("qscale_t", [6, 128, 1]), ("kscale_t", [6, 128, 1]),
            ("lng_t", [6, 128, 8]), ("lnb_t", [6, 128, 8]),
            ("b1a_t", [6, 128, NJ]), ("b1g_t", [6, 128, NJ]),
            ("b2_t", [6, 128, 8]),
            ("ident", [128, 128]), ("mcos", [128, T]), ("msinp", [128, T]),
            ("mrotT", [64, 64]), ("qcos", [128, T]), ("qsinp", [128, T]),
            ("kcos", [128, T]), ("ksinp", [128, T]), ("irotT", [128, 128]),
            ("maskC", [128, 128]), ("maskA", [128, 128]),
            ("onescol", [128, 1]), ("onesrow", [1, 128]), ("ones2", [128, 2]),
            ("E4", [4, 128]), ("E2", [2, 128]), ("E16g", [16, 512]),
        ]
        for nm, shp in shapes:
            D[nm] = nc.dram_tensor(nm, shp, F32, kind="ExternalInput")
        D["x1_st"] = nc.dram_tensor("x1_st", [B_LOC, 128, NCH * T], F32)
        if self.debug == "snap":
            D["snap"] = nc.dram_tensor("snap", [14, 128, NCH * T], BF16,
                                       kind="ExternalOutput")
        D["cattnT_c"] = nc.dram_tensor("cattnT_c", [16, 128, 1024], BF16)
        D["cprojT_c"] = nc.dram_tensor("cprojT_c", [8, 128, 1024], BF16)
        D["wqkv_c"] = nc.dram_tensor("wqkv_c", [6, 6, 128, 4096], BF16)
        D["w1_c"] = nc.dram_tensor("w1_c", [6, NJ, 128, 2048], BF16)
        D["w2_c"] = nc.dram_tensor("w2_c", [6, 8, 128, NJ * 128], BF16)

    def alloc(self, ctx):
        nc = self.nc
        E = ctx.enter_context
        S = self.sb
        S["xbuf"] = E(nc.sbuf_tensor("xbuf", [128, NCH * T], F32))
        S["abf"] = E(nc.sbuf_tensor("abf", [128, AB_SZ], BF16))
        S["af32"] = E(nc.sbuf_tensor("af32", [128, AF_SZ], F32))
        S["tabmc"] = E(nc.sbuf_tensor("tabmc", [128, T], BF16))
        S["tabms"] = E(nc.sbuf_tensor("tabms", [128, T], BF16))
        S["cons_f"] = E(nc.sbuf_tensor("cons_f", [128, 1792], F32))
        S["bias_f"] = E(nc.sbuf_tensor("bias_f", [128, 128], F32))
        S["stat"] = E(nc.sbuf_tensor("stat", [128, 1280], F32))
        S["e16b"] = E(nc.sbuf_tensor("e16b", [16, 512], BF16))
        S["srowb"] = E(nc.sbuf_tensor("srowb", [16, 128], BF16))
        S["bcr"] = E(nc.sbuf_tensor("bcr", [1, 1024], BF16))
        S["onesrb"] = E(nc.sbuf_tensor("onesrb", [1, 128], BF16))
        PS = self.ps
        for i in range(4):
            PS[f"s{i}"] = E(nc.psum_tensor(f"s{i}", [128, 512], F32))
        PS["pv"] = E(nc.psum_tensor("pv", [128, 512], F32))
        PS["acc"] = E(nc.psum_tensor("acc", [128, 1024], F32))
        PS["ptb"] = E(nc.psum_tensor("ptb", [128, 512], BF16))

    def B(self, off, cols, p0=0, pn=128):
        return self.sb["abf"][p0:p0 + pn, off:off + cols]

    def A(self, off, cols, p0=0, pn=128):
        return self.sb["af32"][p0:p0 + pn, off:off + cols]

    def mm(self, out, lhsT, rhs, start, stop, reads, writes, tp=None,
           skip=False):
        return self.P.op("pe", lambda e: e.matmul(
            out, lhsT, rhs, start=start, stop=stop, tile_position=tp,
            skip_group_check=skip),
            reads=reads, writes=writes)

    def fence_pe(self, dump_ps, dump_keys, cover_keys):
        idb = self.B(R_IDENT, 128)
        return self.P.op("pe", lambda e: e.matmul(
            dump_ps, idb, idb[:, 0:1], start=True, stop=True,
            skip_group_check=True),
            reads=K.bf(R_IDENT, 128),
            writes=list(cover_keys) + list(dump_keys))

    def load_consts(self):
        P, D, S = self.P, self.dram, self.sb
        for nm, tens in [("mcos", S["tabmc"]), ("msinp", S["tabms"])]:
            for h in range(2):
                st = self.A(A_ST0, 1024)
                P.dma("sync", lambda e, s=D[nm][:, 1024 * h:1024 * h + 1024],
                      d=st: e.dma_start(d, s), writes=K.f(A_ST0, 1024))
                P.op("dve", lambda e, a=st, d=tens[:, 1024 * h:1024 * h + 1024]:
                     e.tensor_copy(d, a), reads=K.f(A_ST0, 1024),
                     writes=[("tab", nm)])
        for nm, dst in [("ident", S["cons_f"][:, 0:128]),
                        ("E4", S["cons_f"][0:4, 128:256]),
                        ("ones2", S["cons_f"][:, 256:258]),
                        ("onesrow", S["cons_f"][0:1, 384:512]),
                        ("kv2l_b_row", S["cons_f"][0:1, 512:1024]),
                        ("E2", S["cons_f"][0:2, 1024:1152]),
                        ("E16g", S["cons_f"][0:16, 1280:1792])]:
            P.dma("sync", lambda e, s=D[nm], d=dst: e.dma_start(d, s[:]),
                  writes=[("consf", nm)])
        def stage_cast(nm, boff, rows=128, cols=128):
            st = self.A(A_ST0, cols, 0, rows)
            P.dma("sync", lambda e, s=D[nm], d=st: e.dma_start(d, s[:]),
                  writes=K.f(A_ST0, cols))
            P.op("dve", lambda e, a=st, d=self.B(boff, cols, 0, rows):
                 e.tensor_copy(d, a),
                 reads=K.f(A_ST0, cols), writes=K.bf(boff, cols))
        stage_cast("ident", R_IDENT)
        stage_cast("maskC", R_MASKC)
        stage_cast("maskA", R_MASKA)
        stage_cast("irotT", R_IROT)
        stage_cast("mrotT", R_MROT, rows=64, cols=64)
        stage_cast("q2l_bd", R_Q2L, cols=64)
        stage_cast("kv2l_bd", R_KV2L, cols=64)
        stage_cast("wqk_bd", R_WQK, rows=64, cols=64)
        stage_cast("l2o_bd", R_L2O, cols=256)
        stage_cast("onescol", R_ONESC, cols=1)
        stage_cast("ones2", R_ONES2, cols=2)
        P.op("dve", lambda e: e.tensor_copy(
            S["e16b"][:], S["cons_f"][0:16, 1280:1792]),
            reads=[("consf", "E16g")], writes=[("e16b",)])
        P.op("dve", lambda e: e.tensor_copy(
            S["onesrb"][0:1, :], S["cons_f"][0:1, 384:512]),
            reads=[("consf", "onesrow")], writes=[("onesrb",)])
        for nm, col, w in [("q2l_b_t", BC_Q2L, 1), ("kv2l_b_t", BC_KV2L, 1),
                           ("wqk_b_t", BC_WQK, 1), ("l2o_b_t", BC_L2O, 1),
                           ("cproj_b_t", BC_CPROJ, 8),
                           ("cattn_b_t", BC_CATTN, 24)]:
            P.dma("sync", lambda e, s=D[nm],
                  d=S["bias_f"][:, col:col + w]: e.dma_start(d, s[:]),
                  writes=[("bias", col + i) for i in range(w)])

    def load_layer_biases(self, li):
        P, D, S = self.P, self.dram, self.sb
        for nm, col, w in [("qscale_t", BC_QS, 1), ("kscale_t", BC_KS, 1),
                           ("lng_t", BC_LNG, 8), ("lnb_t", BC_LNB, 8),
                           ("b1a_t", BC_B1A, NJ), ("b1g_t", BC_B1G, NJ),
                           ("b2_t", BC_B2, 8)]:
            P.dma("sync", lambda e, s=D[nm][li], d=S["bias_f"][:, col:col + w]:
                  e.dma_start(d, s[:]),
                  writes=[("bias", col + i) for i in range(w)])

    def load_inner_tables(self):
        P, D = self.P, self.dram
        for nm, boff in [("qcos", R_QCOS), ("qsinp", R_QSINP),
                         ("kcos", R_KCOS), ("ksinp", R_KSINP)]:
            for h in range(2):
                st = self.A(A_ST0, 1024)
                P.dma("sync", lambda e, s=D[nm][:, 1024 * h:1024 * h + 1024],
                      d=st: e.dma_start(d, s),
                      writes=K.f(A_ST0, 1024))
                P.op("dve", lambda e, a=st, d=self.B(boff + 1024 * h, 1024):
                     e.tensor_copy(d, a), reads=K.f(A_ST0, 1024),
                     writes=K.bf(boff + 1024 * h, 1024))

    def prepass(self):
        P, D, S, PS = self.P, self.dram, self.sb, self.ps
        consf = [("consf", "ident")]
        for og in range(16):
            rows = 128 * og if og < 8 else 1024 + 128 * (og - 8)
            sslot = A_ST0 if og % 2 == 0 else A_ST1
            st = self.A(sslot, 1024)
            P.dma("sync", lambda e, s=D["cattn"][rows:rows + 128, :], d=st:
                  e.dma_start(d, s), writes=K.f(sslot, 1024))
            wt = P_CT if og % 2 == 0 else P_CT2
            for ic in range(8):
                self.P.op("pe", lambda e, i=ic, s=st: e.transpose(
                    PS["acc"][:, 128 * i:128 * i + 128],
                    s[:, 128 * i:128 * i + 128], S["cons_f"][:, 0:128]),
                    reads=K.f(sslot, 1024) + consf,
                    writes=K.ps("acc", 128 * ic, 128))
            self.fence_pe(PS["s0"][:, 0:1], K.ps("s0"), K.ps("acc", 0, 1024))
            P.op("dve", lambda e, d=self.B(wt, 1024): e.tensor_copy(
                d, PS["acc"][:, 0:1024]),
                reads=K.ps("acc", 0, 1024), writes=K.bf(wt, 1024))
            P.dma("pool", lambda e, s=self.B(wt, 1024), d=D["cattnT_c"][og]:
                  e.dma_start(d[:], s), reads=K.bf(wt, 1024),
                  writes=[("wc", "cattnT", og)])
        for oc in range(8):
            sslot = A_ST0 if oc % 2 == 0 else A_ST1
            st = self.A(sslot, 1024)
            P.dma("sync", lambda e, s=D["cproj"][128 * oc:128 * oc + 128, :],
                  d=st: e.dma_start(d, s), writes=K.f(sslot, 1024))
            wt = P_CT if oc % 2 == 0 else P_CT2
            for ic in range(8):
                self.P.op("pe", lambda e, i=ic, s=st: e.transpose(
                    PS["acc"][:, 128 * i:128 * i + 128],
                    s[:, 128 * i:128 * i + 128], S["cons_f"][:, 0:128]),
                    reads=K.f(sslot, 1024) + consf,
                    writes=K.ps("acc", 128 * ic, 128))
            self.fence_pe(PS["s0"][:, 0:1], K.ps("s0"), K.ps("acc", 0, 1024))
            P.op("dve", lambda e, d=self.B(wt, 1024): e.tensor_copy(
                d, PS["acc"][:, 0:1024]),
                reads=K.ps("acc", 0, 1024), writes=K.bf(wt, 1024))
            P.dma("pool", lambda e, s=self.B(wt, 1024), d=D["cprojT_c"][oc]:
                  e.dma_start(d[:], s), reads=K.bf(wt, 1024),
                  writes=[("wc", "cprojT", oc)])
        if self.depth == 0:
            return
        slots = [A_ST0, A_ST1, 2048, 3072]
        ineng = ["sync", "act"]
        ceng = ["dve", "pool"]
        oeng = ["pool", "pool"]
        ci = 0
        for li in range(self.depth):
            for cb in range(6):
                g = li * 6 + cb
                gb = P_A if g % 2 == 0 else P_B
                for q in range(4):
                    sl = slots[ci % 4]
                    st = self.A(sl, 1024)
                    P.dma(ineng[ci % 2], lambda e,
                          s=D["wqkv_p"][li, cb][:, 1024 * q:1024 * q + 1024],
                          d=st: e.dma_start(d, s), writes=K.f(sl, 1024))
                    dst = gb + 1024 * q
                    P.op(ceng[ci % 2], lambda e, a=st, d=self.B(dst, 1024):
                         e.tensor_copy(d, a),
                         reads=K.f(sl, 1024), writes=K.bf(dst, 1024))
                    ci += 1
                P.dma(oeng[g % 2], lambda e, s=self.B(gb, 4096),
                      d=D["wqkv_c"][li, cb]: e.dma_start(d[:], s),
                      reads=K.bf(gb, 4096), writes=[("wc", "wqkv", li, cb)])
        for li in range(self.depth):
            for j in range(NJ):
                g = li * NJ + j
                gb = P_W1A if g % 2 == 0 else P_W1B
                for q in range(2):
                    sl = slots[ci % 4]
                    st = self.A(sl, 1024)
                    P.dma(ineng[ci % 2], lambda e,
                          s=D["w1_p"][li, j][:, 1024 * q:1024 * q + 1024],
                          d=st: e.dma_start(d, s), writes=K.f(sl, 1024))
                    dst = gb + 1024 * q
                    P.op(ceng[ci % 2], lambda e, a=st, d=self.B(dst, 1024):
                         e.tensor_copy(d, a),
                         reads=K.f(sl, 1024), writes=K.bf(dst, 1024))
                    ci += 1
                P.dma(oeng[g % 2], lambda e, s=self.B(gb, 2048),
                      d=D["w1_c"][li, j]: e.dma_start(d[:], s),
                      reads=K.bf(gb, 2048), writes=[("wc", "w1", li, j)])
        for li in range(self.depth):
            for oc in range(8):
                g = li * 8 + oc
                gb = P_W2A if g % 2 == 0 else P_A
                for q in range(3):
                    w = [1024, 1024, 768][q]
                    sl = slots[ci % 4]
                    st = self.A(sl, w)
                    P.dma(ineng[ci % 2], lambda e,
                          s=D["w2_p"][li, oc][:, 1024 * q:1024 * q + w],
                          d=st: e.dma_start(d, s), writes=K.f(sl, 1024))
                    dst = gb + 1024 * q
                    P.op(ceng[ci % 2], lambda e, a=st, d=self.B(dst, w):
                         e.tensor_copy(d, a),
                         reads=K.f(sl, 1024), writes=K.bf(gb, NJ * 128))
                    ci += 1
                P.dma(oeng[g % 2], lambda e, s=self.B(gb, NJ * 128),
                      d=D["w2_c"][li, oc]: e.dma_start(d[:], s),
                      reads=K.bf(gb, NJ * 128),
                      writes=[("wc", "w2", li, oc)])

    def load_x(self, bi):
        P, D, S, PS = self.P, self.dram, self.sb, self.ps
        consf = [("consf", "ident")]
        for tt in range(NTT):
            sslot = [A_ST0, A_ST1][tt % 2]
            st = self.A(sslot, 1024)
            P.dma("pool", lambda e, s=D["x"][bi, 128 * tt:128 * tt + 128, :],
                  d=st: e.dma_start(d, s), writes=K.f(sslot, 1024))
            for c in range(8):
                self.P.op("pe", lambda e, i=c, s=st: e.transpose(
                    PS["acc"][:, 128 * i:128 * i + 128],
                    s[:, 128 * i:128 * i + 128], S["cons_f"][:, 0:128]),
                    reads=K.f(sslot, 1024) + consf,
                    writes=K.ps("acc", 128 * c, 128))
            self.fence_pe(PS["s0"][:, 0:1], K.ps("s0"), K.ps("acc", 0, 1024))
            xk = []
            for c in range(8):
                xk += K.x(2048 * c + 128 * tt, 128)

            def ev(e, t=tt):
                src = PS["acc"][:, 0:1024].rearrange("p (c t) -> p c t", c=8)
                dst = S["xbuf"][:, :].rearrange("p (c t2) -> p c t2", c=8)
                dst = dst[:, :, 128 * t:128 * t + 128]
                return e.tensor_copy(dst, src)
            P.op("dve", ev, reads=K.ps("acc", 0, 1024), writes=xk)

    def ln_stats(self, src_fn, cast_off, sq_off, patch0=False,
                 patch_src=None):
        P, S, PS = self.P, self.sb, self.ps
        onescol = self.B(R_ONESC, 1)
        ock = K.bf(R_ONESC, 1)
        for c in range(8):
            src, rk = src_fn(c)
            cb = self.B(cast_off, 512)
            sq = self.B(sq_off, 512)
            P.op("dve", lambda e, s=src, d=cb: e.tensor_copy(d, s),
                 reads=rk, writes=K.bf(cast_off, 512))
            P.op("act", lambda e, s=src, d=sq: e.activation(d, s, AF.Square),
                 reads=rk, writes=K.bf(sq_off, 512))
            if patch0 and c >= 4:
                P.op("pool", lambda e, d=cb[:, 0:1]: e.memset(d, 0.0),
                     writes=K.bf(cast_off, 1))
                P.op("pool", lambda e, d=sq[:, 0:1]: e.memset(d, 0.0),
                     writes=K.bf(sq_off, 1))
            elif patch_src is not None and c >= 4:
                pap, pk = patch_src(c)
                P.op("dve", lambda e, s=pap, d=cb[:, 0:1]: e.tensor_copy(d, s),
                     reads=pk, writes=K.bf(cast_off, 1))
                P.op("act", lambda e, s=pap, d=sq[:, 0:1]:
                     e.activation(d, s, AF.Square),
                     reads=pk, writes=K.bf(sq_off, 1))
            self.mm(PS["acc"][0:1, 0:512], onescol, cb,
                    start=(c == 0), stop=(c == 7),
                    reads=K.bf(cast_off, 512) + ock,
                    writes=K.ps("acc", 0, 512))
            self.mm(PS["acc"][0:1, 512:1024], onescol, sq,
                    start=(c == 0), stop=(c == 7),
                    reads=K.bf(sq_off, 512) + ock,
                    writes=K.ps("acc", 512, 512))
        inv_n = 1.0 / 1024.0
        g1, g2 = K.f(A_G1, 512), K.f(A_G2, 512)
        g3, g4 = K.f(A_G3, 512), K.f(A_RB, 512)
        mu = self.A(A_G1, 512, 0, 1)
        v2 = self.A(A_G2, 512, 0, 1)
        rstd = self.A(A_G3, 512, 0, 1)
        negmur = self.A(A_RB, 512, 0, 1)
        self.fence_pe(PS["s2"][:, 1:2], K.ps("s2"), K.ps("acc", 0, 1024))
        P.op("dve", lambda e: e.tensor_scalar_mul(
            mu, PS["acc"][0:1, 0:512], inv_n),
            reads=K.ps("acc", 0, 512), writes=g1)
        P.op("act", lambda e: e.activation(v2, mu, AF.Square),
             reads=g1, writes=g2)
        P.op("dve", lambda e: e.scalar_tensor_tensor(
            v2, PS["acc"][0:1, 512:1024], inv_n, v2, OP.mult, OP.subtract),
            reads=K.ps("acc", 512, 512) + g2, writes=g2)
        P.op("dve", lambda e: e.tensor_scalar_add(v2, v2, EPS),
             reads=g2, writes=g2)
        P.op("dve", lambda e: e.reciprocal(v2, v2), reads=g2, writes=g2)
        P.op("act", lambda e: e.activation(rstd, v2, AF.Sqrt),
             reads=g2, writes=g3)
        P.op("dve", lambda e: e.scalar_tensor_tensor(
            negmur, mu, -1.0, rstd, OP.mult, OP.mult),
            reads=g1 + g3, writes=g4)
        return rstd, negmur, g3, g4

    def ln_bcast(self, rstd, negmur, kr, km, rps, mps, rkey, mkey):
        onesrow = self.sb["cons_f"][0:1, 384:512]
        ork = [("consf", "onesrow")]
        self.mm(rps[:, 0:512], onesrow, rstd, True, True,
                reads=kr + ork, writes=rkey)
        self.mm(mps[:, 0:512], onesrow, negmur, True, True,
                reads=km + ork, writes=mkey)

    def mla(self, bi):
        P, D, S, PS = self.P, self.dram, self.sb, self.ps
        ident_b = self.B(R_IDENT, 128)
        idk = K.bf(R_IDENT, 128)
        consf = [("consf", "ident")]
        kst = self.A(A_ST1, 64)
        P.dma("sync", lambda e, s=D["kv2l_bd"], d=kst: e.dma_start(d, s[:]),
              writes=K.f(A_ST1, 64))
        for p in range(8):
            c = BC_CATTN + 16 + p
            self.mm(PS["acc"][0:1, 64 * p:64 * p + 64],
                    S["bias_f"][:, c:c + 1], kst, True, True,
                    reads=[("bias", c)] + K.f(A_ST1, 64),
                    writes=K.ps("acc", 64 * p, 64))
        P.op("act", lambda e: e.copy(S["stat"][0:1, 640:1152],
                                     PS["acc"][0:1, 0:512]),
             reads=K.ps("acc", 0, 512), writes=[("statv",)])
        P.op("dve", lambda e: e.tensor_tensor(
            S["stat"][0:1, 640:1152], S["stat"][0:1, 640:1152],
            S["cons_f"][0:1, 512:1024], OP.add),
            reads=[("statv",), ("consf", "kv2l_b_row")], writes=[("statv",)])
        for p in range(8):
            st = self.A(A_ST0, 1024)
            P.dma("sync", lambda e, s=D["cattn"][2048 + 128 * p:2176 + 128 * p, :],
                  d=st: e.dma_start(d, s), writes=K.f(A_ST0, 1024))
            vst = self.B(M_H1, 1024)
            P.op("dve", lambda e, s=st, d=vst: e.tensor_copy(d, s),
                 reads=K.f(A_ST0, 1024), writes=K.bf(M_H1, 1024))
            for ic in range(8):
                self.mm(PS["acc"][:, 64 * ic:64 * ic + 64],
                        vst[:, 128 * ic:128 * ic + 128],
                        self.B(R_KV2L, 64), True, True,
                        reads=K.bf(M_H1, 1024) + K.bf(R_KV2L, 64),
                        writes=K.ps("acc", 64 * ic, 64))

            def ev(e, pp=p):
                src = PS["acc"][:, 0:512].rearrange("p (i k) -> p i k", i=8)
                dst = S["abf"][:, M_WVL + 64 * pp:M_WVL + 64 * pp + 512 * 8] \
                    .rearrange("p (i k) -> p i k", i=8)[:, :, 0:64]
                return e.tensor_copy(dst, src)
            P.op("dve", ev, reads=K.ps("acc", 0, 512),
                 writes=K.bf(M_WVL, 4096))
        for st_i in range(NST):
            tcol = 512 * st_i

            def xsrc(c, s=st_i):
                off = 2048 * c + 512 * s
                return (S["xbuf"][:, off:off + 512], K.x(off, 512))
            rstd, negmur, kr, km = self.ln_stats(xsrc, M_QKA, M_QKB)
            self.ln_bcast(rstd, negmur, kr, km, PS["s0"], PS["s1"],
                          K.ps("s0"), K.ps("s1"))
            for c in range(8):
                off = 2048 * c + tcol
                t = self.A(A_ST0, 512)
                P.op("dve", lambda e, s=S["xbuf"][:, off:off + 512], d=t:
                     e.tensor_tensor(d, s, PS["s0"][:, 0:512], OP.mult),
                     reads=K.x(off, 512) + K.ps("s0"), writes=K.f(A_ST0, 512))
                dst = self.B(M_H1 + 512 * c, 512)
                P.op("dve", lambda e, d=dst, s=t: e.tensor_tensor(
                    d, s, PS["s1"][:, 0:512], OP.add),
                    reads=K.f(A_ST0, 512) + K.ps("s1"),
                    writes=[("h1", c)] + K.bf(M_H1 + 512 * c, 512))
            h1keys = [("h1", c) for c in range(8)]
            for ttl in range(4):
                ps = PS["s2"] if ttl % 2 == 0 else PS["s3"]
                pk = K.ps("s2" if ttl % 2 == 0 else "s3")
                for kc in range(8):
                    self.mm(ps[:, 0:512],
                            self.B(M_H1 + 512 * kc + 128 * ttl, 128),
                            self.B(M_WVL + 512 * kc, 512),
                            start=(kc == 0), stop=False,
                            reads=[("h1", kc)] + K.bf(M_H1 + 512 * kc, 512)
                            + K.bf(M_WVL, 4096), writes=pk)
                self.mm(ps[:, 0:512], S["cons_f"][0:1, 384:512],
                        S["stat"][0:1, 640:1152], start=False, stop=True,
                        reads=[("consf", "onesrow"), ("statv",)],
                        writes=pk)
                tt = 4 * st_i + ttl
                dst = self.B(M_VL + 512 * tt, 512)
                P.op("act", lambda e, d=dst, s=ps: e.copy(d, s[:, 0:512]),
                     reads=pk, writes=K.bf(M_VL + 512 * tt, 512))
            for qk in range(2):
                for og in range(8):
                    self._mla_qk_group(bi, st_i, qk, og)

    def _mla_qk_group(self, bi, st_i, qk, og):
        P, D, S, PS = self.P, self.dram, self.sb, self.ps
        tcol = 512 * st_i
        wsl = self.B(M_WST, 1024)
        wk = K.bf(M_WST, 1024)
        P.dma("sync", lambda e, s=D["cattnT_c"][8 * qk + og], d=wsl:
              e.dma_start(d, s[:]), reads=[("wc", "cattnT", 8 * qk + og)],
              writes=wk)
        ps = PS["s2"] if og % 2 == 0 else PS["s3"]
        pk = K.ps("s2" if og % 2 == 0 else "s3")
        for kc in range(8):
            self.mm(ps[:, 0:512], wsl[:, 128 * kc:128 * kc + 128],
                    self.B(M_H1 + 512 * kc, 512),
                    start=(kc == 0), stop=(kc == 7),
                    reads=wk + [("h1", kc)] + K.bf(M_H1 + 512 * kc, 512), writes=pk)
        qbf = self.B(M_QKA, 512)
        bcol = BC_CATTN + 8 * qk + og
        P.op("act", lambda e: e.activation(
            qbf, ps[:, 0:512], AF.Identity,
            bias=S["bias_f"][:, bcol:bcol + 1]),
            reads=pk + [("bias", bcol)], writes=K.bf(M_QKA, 512))
        bd = self.B(R_Q2L if qk == 0 else R_KV2L, 64)
        bdk = K.bf(R_Q2L if qk == 0 else R_KV2L, 64)
        self.mm(PS["pv"][0:64, 0:512], bd, qbf, True, True,
                reads=K.bf(M_QKA, 512) + bdk, writes=K.ps("pv"))
        lat = self.A(A_QLAT, 512, 0, 64)
        lbcol = BC_Q2L if qk == 0 else BC_KV2L
        P.op("act", lambda e: e.activation(
            lat, PS["pv"][0:64, 0:512], AF.Identity,
            bias=S["bias_f"][0:64, lbcol:lbcol + 1]),
            reads=K.ps("pv") + [("bias", lbcol)], writes=K.f(A_QLAT, 512))
        xp = self.B(M_QKB, 512, 0, 64)
        P.op("pool", lambda e: e.tensor_tensor(
            xp, lat, S["tabms"][0:64, tcol:tcol + 512], OP.mult),
            reads=K.f(A_QLAT, 512) + [("tab", "msinp")], writes=K.bf(M_QKB, 512))
        self.mm(PS["pv"][0:64, 0:512], self.B(R_MROT, 64, 0, 64), xp,
                True, True, reads=K.bf(M_QKB, 512) + K.bf(R_MROT, 64),
                writes=K.ps("pv"))
        tc = self.A(A_TC, 512, 0, 64)
        P.op("pool", lambda e: e.tensor_tensor(
            tc, lat, S["tabmc"][0:64, tcol:tcol + 512], OP.mult),
            reads=K.f(A_QLAT, 512) + [("tab", "mcos")], writes=K.f(A_TC, 512))
        self.fence_pe(PS["acc"][:, 1022:1023], K.ps("acc", 512, 512),
                      K.ps("pv"))
        j, base = og // 2, 64 * (og % 2)
        if qk == 0:
            qr = self.B(M_QKA, 512, 0, 64)
            P.op("dve", lambda e: e.tensor_tensor(
                qr, tc, PS["pv"][0:64, 0:512], OP.add),
                reads=K.f(A_TC, 512) + K.ps("pv"), writes=K.bf(M_QKA, 512))
            tp = (0, base) if base else None
            self.mm(PS["pv"][base:base + 64, 0:512],
                    self.B(R_WQK, 64, 0, 64), qr, True, True,
                    reads=K.bf(M_QKA, 512) + K.bf(R_WQK, 64),
                    writes=K.ps("pv"), tp=tp)
            off = M_QF + 2048 * j + tcol
            dst = self.B(off, 512, base, 64)
            P.op("act", lambda e: e.activation(
                dst, PS["pv"][base:base + 64, 0:512], AF.Identity,
                bias=S["bias_f"][base:base + 64, BC_WQK:BC_WQK + 1]),
                reads=K.ps("pv") + [("bias", BC_WQK)],
                writes=K.bf(off, 512))
        else:
            off = M_KL + 2048 * j + tcol
            if base == 0:
                dst = self.B(off, 512, 0, 64)
                P.op("dve", lambda e: e.tensor_tensor(
                    dst, tc, PS["pv"][0:64, 0:512], OP.add),
                    reads=K.f(A_TC, 512) + K.ps("pv"),
                    writes=K.bf(off, 512))
            else:
                tmp = self.B(M_QKA, 512, 0, 64)
                P.op("dve", lambda e: e.tensor_tensor(
                    tmp, tc, PS["pv"][0:64, 0:512], OP.add),
                    reads=K.f(A_TC, 512) + K.ps("pv"),
                    writes=K.bf(M_QKA, 512))
                self.mm(PS["pv"][64:128, 0:512],
                        self.B(R_IDENT, 64, 0, 64), tmp, True, True,
                        reads=K.bf(M_QKA, 512) + K.bf(R_IDENT, 64),
                        writes=K.ps("pv"), tp=(0, 64))
                dst = self.B(off, 512, 64, 64)
                P.op("act", lambda e: e.copy(
                    dst, PS["pv"][64:128, 0:512]),
                    reads=K.ps("pv"), writes=[("kl", og, st_i)])

    def mla_m2(self, bi):
        P, D, S, PS = self.P, self.dram, self.sb, self.ps
        for oc in range(8):
            dst = self.B(M_CPT + 1024 * oc, 1024)
            P.dma("sync", lambda e, s=D["cprojT_c"][oc], d=dst:
                  e.dma_start(d, s[:]), reads=[("wc", "cprojT", oc)],
                  writes=K.bf(M_CPT + 1024 * oc, 1024))
        for st_i in range(NST):
            for qt in range(4 * st_i, 4 * st_i + 4):
                self._mla_qtile(bi, qt)
            self._mla_epilogue(bi, st_i)

    def _mla_qtile(self, bi, qt):
        P, S, PS = self.P, self.sb, self.ps
        ident_b = self.B(R_IDENT, 128)
        idk = K.bf(R_IDENT, 128)
        maskC = self.B(R_MASKC, 128)
        mck = K.bf(R_MASKC, 128)
        nchunk = qt // 4 + 1
        wlast = 128 * (qt + 1) - 512 * (nchunk - 1)
        stat = S["stat"]

        def s_mms(j, c, w, ps_names):
            for hp in range(4):
                lhsT = self.B(M_QF + 2048 * j + 128 * qt, 128, 32 * hp, 32)
                rk = K.bf(M_QF + 2048 * j + 128 * qt, 128) \
                    + K.bf(M_KL + 2048 * j + 512 * c, w)
                ps = PS[ps_names[hp]]
                pk = K.ps(ps_names[hp])
                if c == nchunk - 1:
                    if w > 128:
                        rhs = self.B(M_KL + 2048 * j + 512 * c, w - 128,
                                     32 * hp, 32)
                        self.mm(ps[:, 0:w - 128], lhsT, rhs, True, True,
                                reads=rk, writes=pk, tp=(32 * hp, 0))
                    self.mm(ps[:, w - 128:w], ident_b, maskC, True, False,
                            reads=idk + mck, writes=pk)
                    rhs = self.B(M_KL + 2048 * j + 512 * c + w - 128, 128,
                                 32 * hp, 32)
                    self.mm(ps[:, w - 128:w], lhsT, rhs, False, True,
                            reads=rk + idk + mck, writes=pk, tp=(32 * hp, 0))
                else:
                    rhs = self.B(M_KL + 2048 * j + 512 * c, w, 32 * hp, 32)
                    self.mm(ps[:, 0:w], lhsT, rhs, True, True,
                            reads=rk, writes=pk, tp=(32 * hp, 0))
        psn = ["s0", "s1", "s2", "s3"]
        for j in range(4):
            for c in range(nchunk):
                w = 512 if c < nchunk - 1 else wlast
                s_mms(j, c, w, psn)
                for hp in range(4):
                    dst_c = SC_RUN + hp if c == 0 else SC_TMP + hp
                    dst = stat[:, dst_c:dst_c + 1]
                    P.op("dve", lambda e, d=dst, p=PS[psn[hp]], ww=w:
                         e.reduce_max(d, p[:, 0:ww], axis=AX.X),
                         reads=K.ps(psn[hp]), writes=[("st", dst_c)])
                    if c > 0:
                        rc = SC_RUN + hp
                        P.op("dve", lambda e, a=stat[:, rc:rc + 1],
                             b_=stat[:, SC_TMP + hp:SC_TMP + hp + 1]:
                             e.tensor_tensor(a, a, b_, OP.max),
                             reads=[("st", rc), ("st", SC_TMP + hp)],
                             writes=[("st", rc)])
            for hp in range(4):
                rc, bc = SC_RUN + hp, SC_B4 + hp
                P.op("dve", lambda e, d=stat[:, bc:bc + 1],
                     a=stat[:, rc:rc + 1]: e.tensor_scalar_mul(d, a, -MSC),
                     reads=[("st", rc)], writes=[("st", bc)])
            for c in range(nchunk):
                w = 512 if c < nchunk - 1 else wlast
                s_mms(j, c, w, psn)
                for hp in range(4):
                    pb = self.B(M_PBF + 512 * hp, w)
                    bc = SC_B4 + hp
                    sc = SC_SALL + 4 * j + hp if c == 0 else SC_ACCT + hp
                    P.op("act", lambda e, d=pb, p=PS[psn[hp]], ww=w,
                         b_=stat[:, bc:bc + 1], a=stat[:, sc:sc + 1]:
                         e.activation(d, p[:, 0:ww], AF.Exp, bias=b_,
                                      scale=MSC, accum_out=a),
                         reads=K.ps(psn[hp]) + [("st", bc)],
                         writes=K.bf(M_PBF + 512 * hp, w) + [("st", sc)])
                    if c > 0:
                        tc_ = SC_SALL + 4 * j + hp
                        P.op("dve", lambda e, a=stat[:, tc_:tc_ + 1],
                             b_=stat[:, SC_ACCT + hp:SC_ACCT + hp + 1]:
                             e.tensor_tensor(a, a, b_, OP.add),
                             reads=[("st", tc_), ("st", SC_ACCT + hp)],
                             writes=[("st", tc_)])
                nsub = (w + 127) // 128
                for hp in range(4):
                    for sb_ in range(nsub):
                        ww = min(128, w - 128 * sb_)
                        self.P.op("pe", lambda e, h=hp, s2=sb_, ww=ww:
                                  e.transpose(
                                      PS["ptb"][0:ww, 128 * s2:128 * s2 + 128],
                                      self.B(M_PBF + 512 * h + 128 * s2, ww),
                                      ident_b),
                                  reads=K.bf(M_PBF + 512 * hp, 512) + idk,
                                  writes=K.ps("ptb", 128 * sb_, 128))
                    self.fence_pe(PS[psn[hp]][:, 0:1], K.ps(psn[hp]),
                                  K.ps("ptb", 0, 512))
                    ptk = K.ps("ptb", 0, 128 * nsub)
                    dst = self.B(M_PTB, 128 * nsub)
                    P.op("dve", lambda e, d=dst, ww=128 * nsub:
                         e.tensor_copy(d, PS["ptb"][:, 0:ww]),
                         reads=ptk, writes=K.bf(M_PTB, 512))
                    for sb_ in range(nsub):
                        tt = 4 * c + sb_
                        lhsT = self.B(M_VL + 512 * tt + 32 * (4 * j + hp), 32)
                        rhs = self.B(M_PTB + 128 * sb_, 128)
                        first = (c == 0 and sb_ == 0)
                        last = (c == nchunk - 1 and sb_ == nsub - 1)
                        self.mm(PS["pv"][32 * hp:32 * hp + 32,
                                         128 * j:128 * j + 128],
                                lhsT, rhs, first, last,
                                reads=K.bf(M_VL + 512 * tt, 512)
                                + K.bf(M_PTB, 512),
                                writes=K.ps("pv", 128 * j, 128),
                                tp=(0, 32 * hp), skip=True)
        self.fence_pe(PS["s0"][:, 1:2], K.ps("s0"), K.ps("pv", 0, 512))
        consfi = [("consf", "ident")]
        allst = [("st", SC_SALL + i) for i in range(16)]
        self.P.op("pe", lambda e: e.transpose(
            PS["acc"][:, 0:128],
            S["stat"][:, SC_SALL:SC_SALL + 128],
            S["cons_f"][:, 0:128]),
            reads=allst + consfi, writes=K.ps("acc", 0, 128))
        self.fence_pe(PS["s0"][:, 0:1], K.ps("s0"), K.ps("acc", 0, 128))
        srow = S["stat"][0:16, SC_ST:SC_ST + 128]
        P.op("act", lambda e: e.copy(srow, PS["acc"][0:16, 0:128]),
             reads=K.ps("acc", 0, 128), writes=[("sT", 0)])
        P.op("dve", lambda e: e.reciprocal(srow, srow),
             reads=[("sT", 0)], writes=[("sT", 0)])
        P.op("act", lambda e: e.copy(S["srowb"][:], srow),
             reads=[("sT", 0)], writes=[("srb",)])
        for j in range(4):
            self.mm(PS["acc"][:, 128 * j:128 * j + 128],
                    S["e16b"][:, 128 * j:128 * j + 128],
                    S["srowb"][:], True, True,
                    reads=[("srb",), ("e16b",)],
                    writes=K.ps("acc", 128 * j, 128))
            self.fence_pe(PS["s0"][:, 0:1], K.ps("s0"),
                          K.ps("acc", 128 * j, 128))
            rb = self.A(A_RB, 128)
            P.op("dve", lambda e, d=rb, jj=j: e.tensor_copy(
                d, PS["acc"][:, 128 * jj:128 * jj + 128]),
                reads=K.ps("acc", 128 * j, 128), writes=K.f(A_RB, 128))
            dst = self.B(M_YLAT + 128 * j, 128)
            P.op("dve", lambda e, d=dst, jj=j, r=rb: e.tensor_tensor(
                d, PS["pv"][:, 128 * jj:128 * jj + 128], r, OP.mult),
                reads=K.ps("pv", 128 * j, 128) + K.f(A_RB, 128),
                writes=K.bf(M_YLAT + 128 * j, 128))
        l2ok = K.bf(R_L2O, 256)
        for j in range(4):
            for half in range(2):
                self.mm(PS["acc"][:, 256 * j + 128 * half:
                                  256 * j + 128 * half + 128],
                        self.B(R_L2O + 128 * half, 128),
                        self.B(M_YLAT + 128 * j, 128), True, True,
                        reads=l2ok + K.bf(M_YLAT + 128 * j, 128),
                        writes=K.ps("acc", 256 * j + 128 * half, 128))
            dst = self.B(M_YBF + 256 * j, 256)
            P.op("act", lambda e, d=dst, jj=j: e.activation(
                d, PS["acc"][:, 256 * jj:256 * jj + 256], AF.Identity,
                bias=S["bias_f"][:, BC_L2O:BC_L2O + 1]),
                reads=K.ps("acc", 256 * j, 256) + [("bias", BC_L2O)],
                writes=K.bf(M_YBF + 256 * j, 256))
        for oc in range(8):
            for ic in range(8):
                self.mm(PS["pv"][:, 128 * (oc % 4):128 * (oc % 4) + 128],
                        self.B(M_CPT + 1024 * oc + 128 * ic, 128),
                        self.B(M_YBF + 128 * ic, 128),
                        start=(ic == 0), stop=(ic == 7),
                        reads=K.bf(M_CPT + 1024 * oc + 128 * ic, 128)
                        + K.bf(M_YBF + 128 * ic, 128),
                        writes=K.ps("pv", 128 * (oc % 4), 128))
            dst = self.A(A_ASB + 512 * oc + 128 * (qt % 4), 128)
            bcol = BC_CPROJ + oc
            P.op("act", lambda e, d=dst, o=oc, bc=bcol: e.activation(
                d, PS["pv"][:, 128 * (o % 4):128 * (o % 4) + 128], AF.Identity,
                bias=S["bias_f"][:, bc:bc + 1]),
                reads=K.ps("pv", 128 * (oc % 4), 128) + [("bias", bcol)],
                writes=K.f(A_ASB + 512 * oc + 128 * (qt % 4), 128))

    def _mla_epilogue(self, bi, st_i):
        P, D, S, PS = self.P, self.dram, self.sb, self.ps
        tcol = 512 * st_i

        def asrc(c):
            off = A_ASB + 512 * c
            return (self.A(off, 512), K.f(off, 512))
        rstd, negmur, kr, km = self.ln_stats(asrc, M_PBF, M_PBF + 512)
        self.ln_bcast(rstd, negmur, kr, km, PS["s0"], PS["s1"],
                      K.ps("s0"), K.ps("s1"))
        for c in range(8):
            off = 2048 * c + tcol
            t = self.A(A_ST0, 512)
            P.op("dve", lambda e, s=self.A(A_ASB + 512 * c, 512), d=t:
                 e.tensor_tensor(d, s, PS["s0"][:, 0:512], OP.mult),
                 reads=K.f(A_ASB + 512 * c, 512) + K.ps("s0"),
                 writes=K.f(A_ST0, 512))
            P.op("dve", lambda e, d=t: e.tensor_tensor(
                d, d, PS["s1"][:, 0:512], OP.add),
                reads=K.f(A_ST0, 512) + K.ps("s1"), writes=K.f(A_ST0, 512))
            xs = S["xbuf"][:, off:off + 512]
            P.op("pool", lambda e, x=xs, s=t: e.tensor_tensor(
                x, x, s, OP.add),
                reads=K.x(off, 512) + K.f(A_ST0, 512), writes=K.x(off, 512))
            P.dma("pool", lambda e, x=xs, d=D["x1_st"][bi][:, off:off + 512]:
                  e.dma_start(d, x), reads=K.x(off, 512),
                  writes=[("x1st", bi, off)])
        def xsrc(c, s=st_i):
            off = 2048 * c + 512 * s
            return (S["xbuf"][:, off:off + 512], K.x(off, 512))
        rstd, negmur, kr, km = self.ln_stats(xsrc, M_PBF, M_PBF + 512)
        self.ln_bcast(rstd, negmur, kr, km, PS["s2"], PS["s3"],
                      K.ps("s2"), K.ps("s3"))
        for c in range(8):
            off = 2048 * c + tcol
            t = self.A(A_ST0, 512)
            xs = S["xbuf"][:, off:off + 512]
            P.op("dve", lambda e, s=xs, d=t: e.tensor_tensor(
                d, s, PS["s2"][:, 0:512], OP.mult),
                reads=K.x(off, 512) + K.ps("s2"), writes=K.f(A_ST0, 512))
            P.op("dve", lambda e, d=xs, s=t: e.tensor_tensor(
                d, s, PS["s3"][:, 0:512], OP.add),
                reads=K.f(A_ST0, 512) + K.ps("s3"), writes=K.x(off, 512))

    def inner_attn(self, bi, li):
        P, D, S, PS = self.P, self.dram, self.sb, self.ps
        self.load_layer_biases(li)
        if li == 0:
            self.load_inner_tables()
        for kc in range(8):
            sl = [A_ST0, A_ST1][kc % 2]
            st = self.A(sl, 1024)
            P.dma("sync", lambda e, s=D["wo_p"][li][:, 1024 * kc:1024 * kc + 1024],
                  d=st: e.dma_start(d, s), writes=K.f(sl, 1024))
            dst = self.B(I_WO + 1024 * kc, 1024)
            P.op("dve", lambda e, a=st, d=dst: e.tensor_copy(d, a),
                 reads=K.f(sl, 1024), writes=K.bf(I_WO + 1024 * kc, 1024))
        for st_i in range(NST):
            self._inner_proj(bi, li, st_i)
            for qt in range(4 * st_i, 4 * st_i + 4):
                self._inner_qtile(bi, li, qt)
            self._inner_wo(bi, li, st_i)

    def _inner_proj(self, bi, li, st_i):
        P, D, S, PS = self.P, self.dram, self.sb, self.ps
        tcol = 512 * st_i
        stash = S["stat"][:, 1152 + 4 * (st_i % 2):1156 + 4 * (st_i % 2)]
        for c in range(4, 8):
            off2 = 2048 * c + tcol + 511
            P.op("dve", lambda e, s=S["xbuf"][:, off2:off2 + 1],
                 d=stash[:, c - 4:c - 3]: e.tensor_copy(d, s),
                 reads=K.x(off2, 1), writes=[("stsh", st_i % 2, c)])
        pstash = S["stat"][:, 1152 + 4 * ((st_i + 1) % 2):
                           1156 + 4 * ((st_i + 1) % 2)]
        apost = S["stat"][:, 1169:1173]
        if st_i > 0:
            for c in range(4, 8):
                off3 = 2048 * c + tcol - 1
                P.op("dve", lambda e, s=S["xbuf"][:, off3:off3 + 1],
                     d=apost[:, c - 4:c - 3]: e.tensor_copy(d, s),
                     reads=K.x(off3, 1), writes=[("apost", c)])
                P.op("dve", lambda e, s=pstash[:, c - 4:c - 3],
                     d=S["xbuf"][:, off3:off3 + 1]: e.tensor_copy(d, s),
                     reads=[("stsh", (st_i + 1) % 2, c), ("apost", c)],
                     writes=K.x(off3, 1))
        for c in range(8):
            off = 2048 * c + tcol
            dst = self.B(I_INBF + 512 * c, 512)
            wk = K.bf(I_INBF + 512 * c, 512)
            if c < 4:
                P.op("dve", lambda e, s=S["xbuf"][:, off:off + 512], d=dst:
                     e.tensor_copy(d, s), reads=K.x(off, 512), writes=wk)
            elif st_i > 0:
                P.op("dve", lambda e, s=S["xbuf"][:, off - 1:off + 511], d=dst:
                     e.tensor_copy(d, s),
                     reads=K.x(off - 1, 512), writes=wk)
            else:
                P.op("dve", lambda e, s=S["xbuf"][:, off:off + 511],
                     d=dst[:, 1:512]: e.tensor_copy(d, s),
                     reads=K.x(off, 512), writes=wk)
                P.op("pool", lambda e, d=dst[:, 0:1]: e.memset(d, 0.0),
                     writes=wk)
        if st_i > 0:
            for c in range(4, 8):
                off3 = 2048 * c + tcol - 1
                P.op("dve", lambda e, s=apost[:, c - 4:c - 3],
                     d=S["xbuf"][:, off3:off3 + 1]: e.tensor_copy(d, s),
                     reads=[("apost", c)], writes=K.x(off3, 1))
        inks = [K.bf(I_INBF + 512 * c, 512) for c in range(8)]
        inall = sum(inks, [])
        wb = self.B(I_WQKV, 4096)
        whk = [K.bf(I_WQKV, 2048), K.bf(I_WQKV + 2048, 2048)]
        psl = [PS["s0"], PS["s1"], PS["s2"], PS["s3"]]
        pkl = [K.ps("s0"), K.ps("s1"), K.ps("s2"), K.ps("s3")]
        for half in range(2):
            P.dma("sync", lambda e,
                  s=D["wqkv_c"][li, 0][:, 2048 * half:2048 * half + 2048],
                  d=wb[:, 2048 * half:2048 * half + 2048]: e.dma_start(d, s),
                  reads=[("wc", "wqkv", li, 0)], writes=whk[half])
        for cb in range(6):
            for half in range(2):
                for kc in range(4 * half, 4 * half + 4):
                    for gg in range(4):
                        if cb < 4:
                            self.mm(psl[gg][:, 0:512],
                                    wb[:, 512 * kc + 128 * gg:
                                       512 * kc + 128 * gg + 128],
                                    self.B(I_INBF + 512 * kc, 512),
                                    start=(kc == 0), stop=(kc == 7),
                                    reads=whk[half] + inks[kc],
                                    writes=pkl[gg])
                        else:
                            self.mm(psl[gg][:, 0:512],
                                    self.B(I_INBF + 512 * kc + 128 * gg, 128),
                                    wb[:, 512 * kc:512 * kc + 512],
                                    start=(kc == 0), stop=(kc == 7),
                                    reads=whk[half] + inks[kc],
                                    writes=pkl[gg])
                if cb + 1 < 6:
                    P.dma("sync", lambda e, s=D["wqkv_c"][li, cb + 1]
                          [:, 2048 * half:2048 * half + 2048],
                          d=wb[:, 2048 * half:2048 * half + 2048]:
                          e.dma_start(d, s),
                          reads=[("wc", "wqkv", li, cb + 1)],
                          writes=whk[half])
            if cb < 4:
                for gg in range(4):
                    g = 4 * cb + gg
                    self._inner_rope(li, st_i, g, psl[gg], pkl[gg])
            else:
                for ttl in range(4):
                    tt = 4 * st_i + ttl
                    sl3 = (tt // 2) % 3
                    off = I_VB + 2048 * sl3 + 1024 * (tt % 2) + 512 * (cb - 4)
                    dst = self.B(off, 512)
                    P.op("act", lambda e, d=dst, p=psl[ttl]:
                         e.copy(d, p[:, 0:512]),
                         reads=pkl[ttl], writes=K.bf(off, 512))

    def _inner_rope(self, li, st_i, g, ps, pk):
        P, S, PS = self.P, self.sb, self.ps
        isq = g < 8
        gc = g if isq else g - 8
        tcol = 512 * st_i
        q_sb = self.A(A_ST0, 512)
        qn = self.A(A_ST0 + 512, 512)
        sq = self.B(I_PFB, 512)
        P.op("act", lambda e, d=q_sb, p=ps: e.copy(d, p[:, 0:512]),
             reads=pk, writes=K.f(A_ST0, 512))
        P.op("act", lambda e, d=sq, p=ps: e.activation(d, p[:, 0:512],
                                                       AF.Square),
             reads=pk, writes=K.bf(I_PFB, 512))
        self.mm(PS["acc"][0:2, 0:512], self.B(R_ONES2, 2), sq, True, True,
                reads=K.bf(I_PFB, 512) + K.bf(R_ONES2, 2),
                writes=K.ps("acc", 0, 512))
        nrm = self.A(A_G1, 512, 0, 2)
        P.op("act", lambda e, d=nrm: e.activation(
            d, PS["acc"][0:2, 0:512], AF.Sqrt),
            reads=K.ps("acc", 0, 512), writes=K.f(A_G1, 512))
        P.op("dve", lambda e, d=nrm: e.tensor_scalar(
            d, d, 1e-12, None, OP.max), reads=K.f(A_G1, 512),
            writes=K.f(A_G1, 512))
        P.op("dve", lambda e, d=nrm: e.reciprocal(d, d),
             reads=K.f(A_G1, 512), writes=K.f(A_G1, 512))
        self.mm(PS["pv"][:, 0:512], S["cons_f"][0:2, 1024:1152], nrm,
                True, True, reads=K.f(A_G1, 512) + [("consf", "E2")],
                writes=K.ps("pv"))
        scol = BC_QS if isq else BC_KS
        self.fence_pe(PS["acc"][:, 1022:1023], K.ps("acc", 512, 512),
                      K.ps("pv"))
        P.op("dve", lambda e: e.scalar_tensor_tensor(
            qn, q_sb, S["bias_f"][:, scol:scol + 1], PS["pv"][:, 0:512],
            OP.mult, OP.mult),
            reads=K.f(A_ST0, 512) + [("bias", scol)] + K.ps("pv"),
            writes=K.f(A_ST0 + 512, 512))
        tsin = S["abf"][:, (R_QSINP if isq else R_KSINP) + tcol:
                        (R_QSINP if isq else R_KSINP) + tcol + 512]
        tcos = S["abf"][:, (R_QCOS if isq else R_KCOS) + tcol:
                        (R_QCOS if isq else R_KCOS) + tcol + 512]
        tabk = [("tab", "qsinp" if isq else "ksinp"),
                ("tab", "qcos" if isq else "kcos")]
        xp = self.B(I_PFB, 512)
        P.op("pool", lambda e: e.tensor_tensor(xp, qn, tsin, OP.mult),
             reads=K.f(A_ST0 + 512, 512) + [tabk[0]], writes=K.bf(I_PFB, 512))
        self.mm(ps[:, 0:512], self.B(R_IROT, 128), xp, True, True,
                reads=K.bf(I_PFB, 512) + K.bf(R_IROT, 128), writes=pk)
        t2 = self.A(A_ST0, 512)
        P.op("pool", lambda e: e.tensor_tensor(t2, qn, tcos, OP.mult),
             reads=K.f(A_ST0 + 512, 512) + [tabk[1]], writes=K.f(A_ST0, 512))
        if isq:
            off = I_QB + 512 * gc
        else:
            tt0 = 4 * st_i
            sl3a, sl3b = (tt0 // 2) % 3, (tt0 // 2 + 1) % 3
            offa = I_KB + 2048 * sl3a + 256 * gc
            offb = I_KB + 2048 * sl3b + 256 * gc
        self.fence_pe(PS["acc"][:, 1022:1023], K.ps("acc", 512, 512), pk)
        if isq:
            dst = self.B(off, 512)
            P.op("dve", lambda e, d=dst: e.tensor_tensor(
                d, t2, ps[:, 0:512], OP.add),
                reads=K.f(A_ST0, 512) + pk, writes=K.bf(off, 512))
        else:
            for hh, off in ((0, offa), (1, offb)):
                dst = self.B(off, 256)
                P.op("dve", lambda e, d=dst, h=hh: e.tensor_tensor(
                    d, t2[:, 256 * h:256 * h + 256],
                    ps[:, 256 * h:256 * h + 256], OP.add),
                    reads=K.f(A_ST0, 512) + pk, writes=K.bf(off, 256))

    def _inner_qtile(self, bi, li, qt):
        P, S, PS = self.P, self.sb, self.ps
        ident_b = self.B(R_IDENT, 128)
        idk = K.bf(R_IDENT, 128)
        lo = max(0, qt - 2)
        ntile = qt - lo + 1
        w = 128 * ntile
        stat = S["stat"]
        psn = ["s0", "s1", "s2", "s3"]
        for h in range(16):
            pr, pb = h // 2, h % 2
            ps = PS[psn[h % 4]]
            pk = K.ps(psn[h % 4])
            qoff = I_QB + 512 * pr + 128 * (qt % 4)
            lhsT = self.B(qoff, 128, 64 * pb, 64)
            for i, tt in enumerate(range(lo, qt + 1)):
                sl3 = (tt // 2) % 3
                koff = I_KB + 2048 * sl3 + 256 * pr + 128 * (tt % 2)
                rhs = self.B(koff, 128, 64 * pb, 64)
                mask_off = None
                if i == 0 and qt >= 2:
                    mask_off = R_MASKA
                elif tt == qt:
                    mask_off = R_MASKC
                if mask_off is not None:
                    self.mm(ps[:, 128 * i:128 * i + 128], ident_b,
                            self.B(mask_off, 128), True, False,
                            reads=idk + K.bf(mask_off, 128), writes=pk)
                self.mm(ps[:, 128 * i:128 * i + 128], lhsT, rhs,
                        mask_off is None, True,
                        reads=K.bf(qoff, 128) + K.bf(koff, 128),
                        writes=pk, tp=(64 * pb, 0))
            mc = SC_INMAX + h % 4
            P.op("dve", lambda e, d=stat[:, mc:mc + 1], p=ps, ww=w:
                 e.reduce_max(d, p[:, 0:ww], axis=AX.X),
                 reads=pk, writes=[("st", mc)])
            bc = SC_INBIAS + h % 4
            P.op("dve", lambda e, d=stat[:, bc:bc + 1], a=stat[:, mc:mc + 1]:
                 e.tensor_scalar_mul(d, a, -ISC),
                 reads=[("st", mc)], writes=[("st", bc)])
            pf = self.B(I_PFB + 512 * pb, w)
            sc = SC_INSUM + h % 4
            P.op("act", lambda e, d=pf, p=ps, ww=w, b_=stat[:, bc:bc + 1],
                 a=stat[:, sc:sc + 1]: e.activation(
                     d, p[:, 0:ww], AF.Exp, bias=b_, scale=ISC, accum_out=a),
                 reads=pk + [("st", bc)],
                 writes=K.bf(I_PFB + 512 * pb, 512) + [("st", sc)])
            rc = SC_INRS + h % 4
            P.op("dve", lambda e, d=stat[:, rc:rc + 1], a=stat[:, sc:sc + 1]:
                 e.reciprocal(d, a), reads=[("st", sc)], writes=[("st", rc)])
            P.op("act", lambda e, d=pf, r=stat[:, rc:rc + 1]:
                 e.activation(d, d, AF.Identity, scale=r),
                 reads=K.bf(I_PFB + 512 * pb, 512) + [("st", rc)],
                 writes=K.bf(I_PFB + 512 * pb, 512))
            for i in range(ntile):
                self.P.op("pe", lambda e, ii=i, s=pf: e.transpose(
                    PS["ptb"][:, 128 * ii:128 * ii + 128],
                    s[:, 128 * ii:128 * ii + 128], ident_b),
                    reads=K.bf(I_PFB + 512 * pb, 512) + idk,
                    writes=K.ps("ptb", 128 * i, 128))
            self.fence_pe(ps[:, 0:1], pk, K.ps("ptb", 0, 512))
            dst = self.B(I_PTB, w)
            P.op("dve", lambda e, d=dst, ww=w: e.tensor_copy(
                d, PS["ptb"][:, 0:ww]),
                reads=K.ps("ptb", 0, w), writes=K.bf(I_PTB, 512))
            for i, tt in enumerate(range(lo, qt + 1)):
                sl3 = (tt // 2) % 3
                voff = I_VB + 2048 * sl3 + 1024 * (tt % 2) + 64 * h
                self.mm(PS["pv"][64 * pb:64 * pb + 64,
                                 128 * (pr % 4):128 * (pr % 4) + 128],
                        self.B(voff, 64), self.B(I_PTB + 128 * i, 128),
                        start=(i == 0), stop=(i == ntile - 1),
                        reads=K.bf(voff, 64) + K.bf(I_PTB, 512),
                        writes=K.ps("pv", 128 * (pr % 4), 128),
                        tp=(0, 64 * pb), skip=True)
            if h % 8 == 7:
                grp = h // 8
                self.fence_pe(PS[psn[h % 4]][:, 0:1], K.ps(psn[h % 4]),
                              K.ps("pv", 0, 512))
                base = I_INBF + 2048 * grp + 128 * (qt % 4)

                def yev(e, b=base):
                    src = PS["pv"][:, 0:512].rearrange(
                        "p (c t) -> p c t", c=4)
                    dv = S["abf"][:, b:b + 2048].rearrange(
                        "p (c t) -> p c t", c=4)[:, :, 0:128]
                    return e.tensor_copy(dv, src)
                P.op("dve", yev, reads=K.ps("pv", 0, 512),
                     writes=K.bf(I_INBF + 2048 * grp, 2048))
    def _inner_wo(self, bi, li, st_i):
        P, S, PS = self.P, self.sb, self.ps
        tcol = 512 * st_i
        yk = K.bf(I_INBF, 4096)
        for oc in range(8):
            sl = 512 * (oc % 2)
            for kc in range(8):
                self.mm(PS["acc"][:, sl:sl + 512],
                        self.B(I_WO + 1024 * kc + 128 * oc, 128),
                        self.B(I_INBF + 512 * kc, 512),
                        start=(kc == 0), stop=(kc == 7),
                        reads=K.bf(I_WO + 1024 * kc, 1024) + yk,
                        writes=K.ps("acc", sl, 512))
            off = 2048 * oc + tcol
            xs = S["xbuf"][:, off:off + 512]
            self.fence_pe(PS["s0"][:, 1:2], K.ps("s0"),
                          K.ps("acc", sl, 512))
            P.op("dve", lambda e, x=xs, a=PS["acc"][:, sl:sl + 512]:
                 e.tensor_tensor(x, x, a, OP.add),
                 reads=K.ps("acc", sl, 512) + K.x(off, 512),
                 writes=K.x(off, 512))

    def inner_ffn(self, bi, li):
        P, D, S, PS = self.P, self.dram, self.sb, self.ps
        fst = S["stat"][:, 1161:1165]
        for c in range(4, 8):
            off2 = 2048 * c + 1023
            P.op("dve", lambda e, s=S["xbuf"][:, off2:off2 + 1],
                 d=fst[:, c - 4:c - 3]: e.tensor_copy(d, s),
                 reads=K.x(off2, 1), writes=[("fsth", c)])
        fpost = S["stat"][:, 1165:1169]
        for hf in range(2):
            if hf == 1:
                for c in range(4, 8):
                    off2 = 2048 * c + 1023
                    P.op("dve", lambda e, s=S["xbuf"][:, off2:off2 + 1],
                         d=fpost[:, c - 4:c - 3]: e.tensor_copy(d, s),
                         reads=K.x(off2, 1), writes=[("fpost", c)])
                    P.op("dve", lambda e, s=fst[:, c - 4:c - 3],
                         d=S["xbuf"][:, off2:off2 + 1]: e.tensor_copy(d, s),
                         reads=[("fsth", c), ("fpost", c)],
                         writes=K.x(off2, 1))
            for tg in range(2):
                tcol = 1024 * hf + 512 * tg
                shift0 = (hf == 0 and tg == 0)

                def src(c, t=tcol, s0=shift0):
                    off = 2048 * c + t
                    if c < 4:
                        return (S["xbuf"][:, off:off + 512], K.x(off, 512))
                    return (S["xbuf"][:, off - 1:off + 511], K.x(off - 1, 512))
                rstd, negmur, kr, km = self.ln_stats(
                    src, F_SQ, F_SQ + 512, patch0=shift0)
                self.ln_bcast(rstd, negmur, kr, km, PS["s0"], PS["s1"],
                              K.ps("s0"), K.ps("s1"))
                for c in range(8):
                    off = 2048 * c + tcol
                    t = self.A(A_ST1 + 512, 512)
                    if c < 4:
                        sap, sk = S["xbuf"][:, off:off + 512], K.x(off, 512)
                    else:
                        sap, sk = S["xbuf"][:, off - 1:off + 511], \
                            K.x(off - 1, 512)
                    P.op("dve", lambda e, s=sap, d=t: e.tensor_tensor(
                        d, s, PS["s0"][:, 0:512], OP.mult),
                        reads=sk + K.ps("s0"), writes=K.f(A_ST1 + 512, 512))
                    P.op("dve", lambda e, d=t: e.tensor_tensor(
                        d, d, PS["s1"][:, 0:512], OP.add),
                        reads=K.f(A_ST1 + 512, 512) + K.ps("s1"),
                        writes=K.f(A_ST1 + 512, 512))
                    xoff = F_XN + 1024 * c + 512 * tg
                    dst = self.B(xoff, 512)
                    gcol, bcol = BC_LNG + c, BC_LNB + c
                    P.op("dve", lambda e, d=dst, s=t, g=gcol, b_=bcol:
                         e.tensor_scalar(d, s, S["bias_f"][:, g:g + 1],
                                         S["bias_f"][:, b_:b_ + 1],
                                         OP.mult, OP.add),
                         reads=K.f(A_ST1 + 512, 512) + [("bias", gcol),
                                                        ("bias", bcol)],
                         writes=K.bf(xoff, 512))
                    if shift0 and c >= 4:
                        P.op("dve", lambda e, d=dst[:, 0:1], g=gcol, b_=bcol:
                             e.tensor_scalar(d, PS["s1"][:, 0:1],
                                             S["bias_f"][:, g:g + 1],
                                             S["bias_f"][:, b_:b_ + 1],
                                             OP.mult, OP.add),
                             reads=K.ps("s1") + [("bias", gcol),
                                                 ("bias", bcol)],
                             writes=K.bf(xoff, 1))

            if hf == 1:
                for c in range(4, 8):
                    off2 = 2048 * c + 1023
                    P.op("dve", lambda e, s=fpost[:, c - 4:c - 3],
                         d=S["xbuf"][:, off2:off2 + 1]: e.tensor_copy(d, s),
                         reads=[("fpost", c)], writes=K.x(off2, 1))
            xnks = [K.bf(F_XN + 1024 * c, 1024) for c in range(8)]
            wb = self.B(F_W1, 2048)
            whk = [K.bf(F_W1, 1024), K.bf(F_W1 + 1024, 1024)]
            for half in range(2):
                P.dma("sync", lambda e, h=half,
                      s=D["w1_c"][li, 0][:, 1024 * half:1024 * half + 1024],
                      d=wb[:, 1024 * half:1024 * half + 1024]:
                      e.dma_start(d, s),
                      reads=[("wc", "w1", li, 0)], writes=whk[half])
            pg = [PS["s0"], PS["s1"]]
            pgk = [K.ps("s0"), K.ps("s1")]
            for j in range(NJ):
                if j % 2 == 0:
                    pa = [PS["s2"], PS["s3"]]
                    pak = [K.ps("s2"), K.ps("s3")]
                else:
                    pa = [PS["acc"][:, 0:512], PS["acc"][:, 512:1024]]
                    pak = [K.ps("acc", 0, 512), K.ps("acc", 512, 512)]
                for half in range(2):
                    for kc in range(4 * half, 4 * half + 4):
                        for tq in range(2):
                            self.mm(pa[tq][:, 0:512],
                                    wb[:, 256 * kc:256 * kc + 128],
                                    self.B(F_XN + 1024 * kc + 512 * tq, 512),
                                    start=(kc == 0), stop=(kc == 7),
                                    reads=whk[half] + xnks[kc],
                                    writes=pak[tq])
                        for tq in range(2):
                            self.mm(pg[tq][:, 0:512],
                                    wb[:, 256 * kc + 128:256 * kc + 256],
                                    self.B(F_XN + 1024 * kc + 512 * tq, 512),
                                    start=(kc == 0), stop=(kc == 7),
                                    reads=whk[half] + xnks[kc],
                                    writes=pgk[tq])
                    if j + 1 < NJ:
                        P.dma("sync", lambda e, h=half,
                              s=D["w1_c"][li, j + 1][:, 1024 * half:
                                                     1024 * half + 1024],
                              d=wb[:, 1024 * half:1024 * half + 1024]:
                              e.dma_start(d, s),
                              reads=[("wc", "w1", li, j + 1)],
                              writes=whk[half])
                for tq in range(2):
                    gl = self.B(F_GELU + 512 * tq, 512)
                    glk = K.bf(F_GELU + 512 * tq, 512)
                    gc = BC_B1G + j
                    P.op("act", lambda e, d=gl, g=gc, p=pg[tq]: e.activation(
                        d, p[:, 0:512], AF.Gelu,
                        bias=S["bias_f"][:, g:g + 1]),
                        reads=pgk[tq] + [("bias", gc)], writes=glk)
                    hoff = F_H1G + 1024 * j + 512 * tq
                    dst = self.B(hoff, 512)
                    ac = BC_B1A + j
                    P.op("dve", lambda e, d=dst, g=gl, a=ac, p=pa[tq]:
                         e.scalar_tensor_tensor(
                             d, p[:, 0:512],
                             S["bias_f"][:, a:a + 1], g, OP.add, OP.mult),
                         reads=pak[tq] + [("bias", ac)] + glk,
                         writes=K.bf(hoff, 512))
            for oc in range(8):
                wsl = F_W2A if oc % 2 == 0 else F_W2B
                wb = self.B(wsl, NJ * 128)
                wbk = K.bf(wsl, NJ * 128)
                P.dma("sync", lambda e, s=D["w2_c"][li, oc], d=wb:
                      e.dma_start(d, s[:]), reads=[("wc", "w2", li, oc)],
                      writes=wbk)
                for tq in range(2):
                    if oc % 2 == 0:
                        ps = PS["s2"] if tq == 0 else PS["s3"]
                        pk = K.ps("s2" if tq == 0 else "s3")
                    else:
                        ps = PS["acc"][:, 512 * tq:512 * tq + 512]
                        pk = K.ps("acc", 512 * tq, 512)
                    for kc in range(NJ):
                        r = min(128, DI - 128 * kc)
                        self.mm(ps[:, 0:512], wb[0:r, 128 * kc:128 * kc + 128],
                                self.B(F_H1G + 1024 * kc + 512 * tq, 512,
                                       0, r),
                                start=(kc == 0), stop=(kc == NJ - 1),
                                reads=wbk + K.bf(F_H1G + 1024 * kc, 1024),
                                writes=pk)
                    off = 2048 * oc + 1024 * hf + 512 * tq
                    xs = S["xbuf"][:, off:off + 512]
                    bc = BC_B2 + oc
                    self.fence_pe(PS["pv"][:, 1:2], K.ps("pv"), pk)
                    P.op("dve", lambda e, x=xs, p=ps, b_=bc:
                         e.scalar_tensor_tensor(
                             x, p[:, 0:512], S["bias_f"][:, b_:b_ + 1], x,
                             OP.add, OP.add),
                         reads=pk + [("bias", bc)] + K.x(off, 512),
                         writes=K.x(off, 512))

    def finale(self, bi):
        P, D, S, PS = self.P, self.dram, self.sb, self.ps
        consfi = [("consf", "ident")]
        for st_i in range(NST):
            tcol = 512 * st_i
            if self.debug != "h2":
                def fsrc(c, s=st_i):
                    off = 2048 * c + 512 * s
                    return (S["xbuf"][:, off:off + 512], K.x(off, 512))
                rstd, negmur, kr, km = self.ln_stats(fsrc, M_PBF, M_PBF + 512)
                self.ln_bcast(rstd, negmur, kr, km, PS["s0"], PS["s1"],
                              K.ps("s0"), K.ps("s1"))
                for c in range(8):
                    off = 2048 * c + tcol
                    dst = self.A(A_ASB + 512 * c, 512)
                    P.dma("sync", lambda e, s=D["x1_st"][bi][:, off:off + 512],
                          d=dst: e.dma_start(d, s),
                          reads=[("x1st", bi, off)],
                          writes=K.f(A_ASB + 512 * c, 512))
                for c in range(8):
                    off = 2048 * c + tcol
                    t = self.A(A_ST1 + 512, 512)
                    xs = S["xbuf"][:, off:off + 512]
                    P.op("dve", lambda e, s=xs, d=t: e.tensor_tensor(
                        d, s, PS["s0"][:, 0:512], OP.mult),
                        reads=K.x(off, 512) + K.ps("s0"),
                        writes=K.f(A_ST1 + 512, 512))
                    P.op("dve", lambda e, d=t: e.tensor_tensor(
                        d, d, PS["s1"][:, 0:512], OP.add),
                        reads=K.f(A_ST1 + 512, 512) + K.ps("s1"),
                        writes=K.f(A_ST1 + 512, 512))
                    P.op("pool", lambda e, x=xs, s=t,
                         x1=self.A(A_ASB + 512 * c, 512): e.tensor_tensor(
                             x, x1, s, OP.add),
                         reads=K.f(A_ST1 + 512, 512)
                         + K.f(A_ASB + 512 * c, 512), writes=K.x(off, 512))
            for ttl in range(4):
                tt = 4 * st_i + ttl
                for c in range(8):
                    off = 2048 * c + 128 * tt
                    self.P.op("pe", lambda e, o=off, i=c: e.transpose(
                        PS["acc"][:, 128 * i:128 * i + 128],
                        S["xbuf"][:, o:o + 128], S["cons_f"][:, 0:128]),
                        reads=K.x(off, 128) + consfi,
                        writes=K.ps("acc", 128 * c, 128))
                self.fence_pe(PS["s0"][:, 0:1], K.ps("s0"),
                              K.ps("acc", 0, 1024))
                sl = [A_ST0, A_ST1][ttl % 2]
                xt = self.A(sl, 1024)
                P.op("dve", lambda e, d=xt: e.tensor_copy(
                    d, PS["acc"][:, 0:1024]),
                    reads=K.ps("acc", 0, 1024), writes=K.f(sl, 1024))
                od = P.dma("pool", lambda e, s=xt,
                           d=D["out"][bi, 128 * tt:128 * tt + 128, :]:
                           e.dma_start(d, s), reads=K.f(sl, 1024))
                self.outdmas.append(od)


def emit_all(b):
    b.load_consts()
    b.prepass()
    barrier = [b.P.ops[e][-1] for e in ENGINES if b.P.ops[e]]
    b.P.op("pool", lambda e: e.memset(b.sb["stat"][0:1, 1278:1279], 0.0),
           after=barrier, writes=[("prebar",)])
    b.P.global_after = b.P.ops["pool"][-1]
    def snap(idx):
        if b.debug != "snap":
            return
        S = b.sb
        tmp = b.B(0, 512)

        def em(c):
            off = 2048 * c
            b.P.op("dve", lambda e, o=off: e.tensor_copy(
                b.sb["abf"][:, 0:2048], S["xbuf"][:, o:o + 2048]),
                reads=K.x(off, 2048), writes=K.bf(0, 2048))
            b.P.dma("pool", lambda e, o=off: e.dma_start(
                b.dram["snap"][idx][:, o:o + 2048], b.sb["abf"][:, 0:2048]),
                reads=K.bf(0, 2048), writes=[("snapw", idx, c)])
        for c in range(8):
            em(c)
    def layer_barrier():
        bar_ops = [b.P.ops[e][-1] for e in ENGINES if b.P.ops[e]]
        b.P.op("pool", lambda e: e.memset(
            b.sb["stat"][0:1, 1278:1279], 0.0), after=bar_ops)
        b.P.global_after = b.P.ops["pool"][-1]
    for bi in range(B_LOC):
        b.load_x(bi)
        b.mla(bi)
        b.mla_m2(bi)
        layer_barrier()
        if bi == 0:
            snap(0)
        for li in range(b.depth):
            b.inner_attn(bi, li)
            layer_barrier()
            if bi == 0:
                snap(1 + 2 * li)
            b.inner_ffn(bi, li)
            layer_barrier()
            if bi == 0:
                snap(2 + 2 * li)
        b.finale(bi)
        layer_barrier()
    b.P.op("pool", lambda e: e.memset(b.sb["stat"][0:1, 1279:1280], 0.0),
           after=b.outdmas)


def build_core(depth=I_DEPTH, debug=None):
    nc = bass.Bass("TRN2", target_bir_lowering=False)
    b = Bld(nc, None, depth=depth, debug=debug)
    b.declare()
    with ExitStack() as ctx:
        P = Plan(nc)
        b.P = P
        sems = [ctx.enter_context(nc.semaphore(f"sm{i}")) for i in range(24)]
        P.give_sems(sems)
        b.alloc(ctx)
        block = ctx.enter_context(nc.Block())
        emit_all(b)
        P.lower(block)
    return nc


_NC_CACHE = {}


def kernel(**inputs):
    per_core = [_relayout_inputs(inputs, c) for c in range(NCORES)]
    key = ("full", I_DEPTH)
    if key not in _NC_CACHE:
        _NC_CACHE[key] = build_core()
    nc = _NC_CACHE[key]
    res = run_bass_kernel_spmd(nc, per_core, core_ids=list(range(NCORES)))
    outs = [r["out"] for r in res.results]
    return np.concatenate(outs, axis=0).reshape(B_TOT, T, C)
